# version 09
# baseline (speedup 1.0000x reference)
import contextlib
import numpy as np
import concourse.bass as bass
import concourse.mybir as mybir
from concourse.bass_utils import run_bass_kernel_spmd

F32 = mybir.dt.float32
BF16 = mybir.dt.bfloat16
OP = mybir.AluOpType
AF = mybir.ActivationFunctionType
AX = mybir.AxisListType

B, T, H = 4, 4096, 1024
TO = T // 2
E = 2048
S = 128
NUV = 2 * E + S
F1 = 4096
MAGIC = 12582912.0
EPS = 1e-5
INV_SQRT_S = float(np.float32(1.0) / np.float32(np.sqrt(np.float32(float(S)))))
NT = T // 128
NTO = TO // 128
BIG = 1.0e30

N1 = B * T * H
R1 = float(N1 - 1 - 16693329)
N2 = B * T * F1
R2 = float(N2 - 1 - 66773320)
SS1_RA = float((int(R1) + 1) // 16 + 500)
SS1_RB = float((int(R1) + 1) // 16 - 500)
SS2_RA = float((int(R2) + 1) // 32 + 750)
SS2_RB = float((int(R2) + 1) // 32 - 750)

DEBUG = []
PHASES = 99


class _Trunc(Exception):
    pass


class _Draining:
    _PASS = {"wait_ge", "drain", "dma_start", "dma_start_transpose",
             "then_inc", "nop", "engine_nop"}

    def __init__(self, eng):
        self._e = eng

    def __getattr__(self, name):
        f = getattr(self._e, name)
        if name in self._PASS or not callable(f):
            return f
        def wrapped(*a, **k):
            self._e.drain()
            return f(*a, **k)
        return wrapped


def build_nc():
    nc = bass.Bass("TRN2", target_bir_lowering=False,
                   detect_race_conditions=False)
    es = contextlib.ExitStack()

    def din(name, shape, dt=F32):
        return nc.dram_tensor(name, list(shape), dt, kind="ExternalInput").ap()

    x_in = din("x", (T, H))
    uv_wT_d = din("uv_wT", (H, NUV))
    uv_b_d = din("uv_b", (1, NUV))
    ln_w_d = din("ln_w", (1, H))
    ln_b_d = din("ln_b", (1, H))
    ffn_g_d = din("ffn_g", (1, H))
    ffn_b_d = din("ffn_b", (1, H))
    gamma_d = din("gamma", (2, S))
    beta_d = din("beta", (2, S))
    o_wT_d = din("o_wT", (E, H))
    o_b_d = din("o_b", (1, H))
    w1T_d = din("w1T", (H, F1))
    w2T_d = din("w2T", (F1, H))
    sincos_d = din("sincos", (2 * T, 64))
    bias_d = din("bias_t", (8 * 32, 128, 256))
    identf_d = din("identf", (128, 128))
    iota_d = din("iota16", (1, 16))

    out_d = nc.dram_tensor("out", [TO, H], F32, kind="ExternalOutput").ap()

    def scr(name, shape, dt=F32):
        if name in DEBUG:
            return nc.dram_tensor(name, list(shape), dt,
                                  kind="ExternalOutput").ap()
        return nc.dram_tensor(name, list(shape), dt).ap()

    xn_sp = scr("xn_sp", (TO, H))
    xn2h_sp = scr("xn2h_sp", (T, H), BF16)
    xn2l_sp = scr("xn2l_sp", (T, H), BF16)
    h1_sp = scr("h1_sp", (TO, H))
    vh_sp = scr("vh_sp", (T, E), BF16)
    vl_sp = scr("vl_sp", (T, E), BF16)
    u_sp = scr("u_sp", (TO, E))
    h2_sp = scr("h2_sp", (F1 // 128, 128, TO))
    gh_sp = scr("gh_sp", (TO, E), BF16)
    gl_sp = scr("gl_sp", (TO, E), BF16)
    fT_sp = scr("fT_sp", (8, 128, TO))
    qrow_sp = scr("qrow_sp", (1, 32))
    qdbg1_sp = scr("qdbg1_sp", (1, 32))
    qdbg2_sp = scr("qdbg2_sp", (1, 32))
    qdbgm_sp = scr("qdbgm_sp", (128, 4))
    ssdbg_sp = scr("ssdbg_sp", (128, 2048))
    ssdbg0_sp = scr("ssdbg0_sp", (128, 1024))
    cc_in = nc.dram_tensor("cc_in", [64, 32], F32).ap()
    cc_out = nc.dram_tensor("cc_out", [64, 32], F32).ap()

    class Sem:
        def __init__(self, name, dma=False):
            self.h = es.enter_context(nc.semaphore(name))
            self.n = 0
            self.step = 16 if dma else 1

        def inc(self, instr):
            instr.then_inc(self.h, self.step)
            self.n += self.step
            return self.n

        def wait(self, eng, val=None):
            v = self.n if val is None else val
            assert v <= self.n, "wait on future value"
            if v > 0:
                eng.wait_ge(self.h, v)

    sSY = Sem("sSY", dma=True)
    sSQ = Sem("sSQ", dma=True)
    sGQ = Sem("sGQ", dma=True)
    sV = Sem("sV")
    sSC = Sem("sSC")
    sPE = Sem("sPE")
    sGP = Sem("sGP")
    sCC = Sem("sCC")

    def sb(name, shape, dt=F32, stk=None):
        return (stk or es).enter_context(nc.sbuf_tensor(name, list(shape), dt))

    def pst(name, shape, stk, dt=F32):
        return stk.enter_context(nc.psum_tensor(name, list(shape), dt))

    ident = sb("ident", (128, 128))
    identb = sb("identb", (128, 128), BF16)
    iot = sb("iot", (1, 16))
    l2a = sb("l2a", (128, 256))
    l2b = sb("l2b", (128, 256))
    l2c = sb("l2c", (128, 256))
    l2e = sb("l2e", (128, 256))
    l2c2 = sb("l2c2", (128, 256))
    qs = sb("qs", (128, 16))
    qrow = sb("qrow", (1, 32))
    trow = sb("trow", (1, 16))
    crow = sb("crow", (1, 32))
    crow2 = sb("crow2", (1, 32))
    tbc = sb("tbc", (128, 16))
    tbcA = sb("tbcA", (128, 16))
    tbcB = sb("tbcB", (128, 16))
    acc = sb("acc", (128, 32))
    sgn = sb("sgn", (128, 32))
    junk = sb("junk", (128, 2048), BF16)
    gsc = sb("gsc", (128, 4))
    mvq = sb("mvq", (128, 2))
    fnv = sb("fnv", (1, 16))
    fng = sb("fng", (1, 16))
    VD = _Draining(nc.vector)
    SD = _Draining(nc.scalar)

    open_stacks = []

    def _close(stk):
        if stk in open_stacks:
            open_stacks.remove(stk)
        stk.close()

    def maybe_stop(k):
        if PHASES < k:
            raise _Trunc()

    with nc.Block():
      try:
        sSY.inc(nc.sync.dma_start(out=ident[:, :], in_=identf_d[:, :]))
        sSY.inc(nc.sync.dma_start(out=iot[:, :], in_=iota_d[:, :]))
        sSY.wait(nc.vector)
        sV.inc(VD.tensor_copy(identb[:, :], ident[:, :]))
        sV.inc(VD.memset(acc[:, :], 0.0))
        sV.inc(VD.memset(sgn[:, :], 0.0))
        sV.inc(VD.memset(qrow[:, :], 0.0))
        sV.inc(VD.memset(trow[:, :], 0.0))
        sV.inc(VD.memset(crow[:, :], 0.0))
        sV.inc(VD.memset(crow2[:, :], 0.0))
        sV.inc(VD.memset(qs[:, :], 0.0))
        sV.inc(VD.memset(mvq[:, :], 0.0))
        sV.inc(VD.memset(gsc[:, :], 0.0))
        sV.inc(VD.memset(fnv[:, :], 0.0))

        def row_to_dram(src_row_ap, width):
            ev = sV.inc(VD.tensor_scalar(fnv[:, 0:8], fnv[:, 0:8], 0.0, None, OP.add))
            sV.wait(nc.gpsimd, ev)
            sGQ.inc(nc.gpsimd.dma_start(out=qrow_sp[:, 0:width],
                                        in_=src_row_ap))
            sGQ.wait(nc.gpsimd)

        def dram_bcast(dst_ap, width):
            ev = sGQ.inc(nc.gpsimd.dma_start(
                out=dst_ap, in_=qrow_sp[:, 0:width].to_broadcast((128, width))))
            return ev

        cc_idx = [0]

        def cc_start(width, op, src_row):
            i = cc_idx[0]
            cc_idx[0] += 1
            assert i < 64
            ev = sV.inc(VD.tensor_scalar(fnv[:, 0:8], fnv[:, 0:8], 0.0, None, OP.add))
            sV.wait(nc.gpsimd, ev)
            nc.gpsimd.wait_ge(sGP.h, sGP.n)
            sGQ.inc(nc.gpsimd.dma_start(out=cc_in[i:i + 1, 0:32],
                                        in_=crow2[:, 0:32]))
            sGQ.wait(nc.gpsimd)
            ec = sCC.inc(nc.gpsimd.collective_compute(
                "AllReduce", op, replica_groups=[list(range(8))],
                ins=[cc_in[i:i + 1, :].opt()], outs=[cc_out[i:i + 1, :].opt()]))
            sCC.wait(nc.gpsimd, ec)
            return sGQ.inc(nc.gpsimd.dma_start(
                out=crow[:, 0:width], in_=cc_out[i:i + 1, 0:width]))

        def gen_thresholds(lo_i, hi_i, nthr, off, denom):
            sV.inc(VD.tensor_tensor(
                qrow[:, 2:3], qrow[:, hi_i:hi_i + 1], qrow[:, lo_i:lo_i + 1],
                OP.subtract))
            sV.inc(VD.tensor_scalar(
                qrow[:, 2:3], qrow[:, 2:3], 1.0 / denom, None, OP.mult))
            sV.inc(VD.tensor_scalar(
                trow[:, off:off + nthr], iot[:, 0:nthr], qrow[:, 2:3],
                qrow[:, lo_i:lo_i + 1], OP.mult, OP.add))

        def update_bracket(lo_i, hi_i, cnt_sl, thr_sl, r_ap):
            n = thr_sl.stop - thr_sl.start
            if isinstance(r_ap, float):
                sV.inc(VD.tensor_scalar(
                    crow2[:, 0:n], crow[:, cnt_sl], r_ap, None, OP.is_gt))
            else:
                sV.inc(VD.tensor_scalar(
                    crow2[:, 0:n], crow[:, cnt_sl], r_ap, None, OP.is_gt))
            sV.inc(VD.tensor_tensor(
                crow2[:, 16:16 + n], trow[:, thr_sl], crow2[:, 0:n], OP.mult))
            sV.inc(VD.tensor_reduce(
                qrow[:, 3:4], crow2[:, 16:16 + n], AX.X, OP.max))
            sV.inc(VD.tensor_tensor(
                qrow[:, lo_i:lo_i + 1], qrow[:, lo_i:lo_i + 1], qrow[:, 3:4],
                OP.max))
            sV.inc(VD.tensor_scalar(
                crow2[:, 0:n], crow2[:, 0:n], BIG, None, OP.mult))
            sV.inc(VD.tensor_tensor(
                crow2[:, 16:16 + n], trow[:, thr_sl], crow2[:, 0:n], OP.add))
            sV.inc(VD.tensor_reduce(
                qrow[:, 3:4], crow2[:, 16:16 + n], AX.X, OP.min))
            sV.inc(VD.tensor_tensor(
                qrow[:, hi_i:hi_i + 1], qrow[:, hi_i:hi_i + 1], qrow[:, 3:4],
                OP.min))

        def quantile_chain(ssbuf, sswidth, ra, rb, rtop, mloc,
                           l1max, l1min, l1cnt, n1c, l2g,
                           capture_tiles, s_cols, g_col, xscr,
                           dbg_sp=None):
            ev = sV.inc(VD.tensor_scalar(
                junk[:, 0:sswidth], ssbuf[:, 0:sswidth], 0.0, None,
                OP.add, OP.max, accum_out=mvq[:, 0:1]))
            sV.wait(nc.gpsimd, ev)
            eg = sGP.inc(nc.gpsimd.tensor_reduce(crow2[:, 0:1], mvq[:, 0:1],
                                                 AX.C, OP.max))
            if dbg_sp is not None:
                sGP.wait(nc.gpsimd, eg)
                sGQ.inc(nc.gpsimd.dma_start(out=ssdbg_sp[:, 0:sswidth],
                                            in_=ssbuf[:, 0:sswidth]))
                sGQ.inc(nc.gpsimd.dma_start(
                    out=qdbgm_sp[:, 0:1], in_=mvq[:, 0:1],
                    allow_slow_non_contiguous=True))
                sGQ.inc(nc.gpsimd.dma_start(
                    out=qdbgm_sp[0:1, 1:2], in_=crow2[:, 0:1],
                    allow_slow_non_contiguous=True))
            g = cc_start(1, OP.max, crow2[:, 0:1])
            yield
            sGQ.wait(nc.vector, g)
            sV.inc(VD.tensor_scalar(
                qrow[:, 7:8], crow[:, 0:1], 1.001, 1e-4, OP.mult, OP.add))
            sV.inc(VD.tensor_scalar(
                qrow[:, 14:15], qrow[:, 7:8], 0.0, None, OP.add))
            sV.inc(VD.memset(qrow[:, 6:7], 0.0))
            sV.inc(VD.tensor_scalar(qrow[:, 8:9], qrow[:, 6:7], 0.0, None, OP.add))
            sV.inc(VD.tensor_scalar(qrow[:, 9:10], qrow[:, 7:8], 0.0, None, OP.add))
            for _ in range(5):
                gen_thresholds(6, 7, 8, 0, 9.0)
                gen_thresholds(8, 9, 8, 8, 9.0)
                row_to_dram(trow[:, 0:16], 16)
                eb = dram_bcast(tbc[:, 0:16], 16)
                sGQ.wait(nc.vector, eb)
                for j in range(16):
                    sV.inc(VD.tensor_scalar(
                        junk[:, 0:sswidth], ssbuf[:, 0:sswidth],
                        tbc[:, j:j + 1], None, OP.is_gt, OP.add,
                        accum_out=acc[:, j:j + 1]))
                    if j % 4 == 3:
                        yield
                ev = sV.inc(VD.tensor_scalar(fnv[:, 0:8], fnv[:, 0:8], 0.0, None, OP.add))
                sV.wait(nc.gpsimd, ev)
                sGP.inc(nc.gpsimd.tensor_reduce(crow2[:, 0:16], acc[:, 0:16],
                                                AX.C, OP.add))
                g = cc_start(16, OP.add, crow2[:, 0:16])
                yield
                sGQ.wait(nc.vector, g)
                update_bracket(6, 7, slice(0, 8), slice(0, 8), ra)
                update_bracket(8, 9, slice(8, 16), slice(8, 16), rb)
            sV.inc(VD.tensor_scalar(trow[:, 0:1], qrow[:, 6:7], 0.0, None, OP.add))
            sV.inc(VD.tensor_scalar(trow[:, 1:2], qrow[:, 9:10], 0.0, None, OP.add))
            sV.inc(VD.tensor_scalar(
                trow[:, 2:3], qrow[:, 9:10], -1.0, None, OP.mult))
            sV.inc(VD.tensor_scalar(qrow[:, 12:13], qrow[:, 6:7], 0.0, None, OP.add))
            sV.inc(VD.tensor_scalar(qrow[:, 13:14], qrow[:, 9:10], 0.0, None, OP.add))
            row_to_dram(trow[:, 0:3], 3)
            eb = dram_bcast(qs[:, 0:3], 3)
            sGQ.wait(nc.vector, eb)
            sGQ.wait(nc.scalar, eb)
            sV.inc(VD.memset(sgn[:, :], 0.0))
            ev = sV.inc(VD.tensor_scalar(fnv[:, 0:8], fnv[:, 0:8], 0.0, None, OP.add))
            sV.wait(nc.scalar, ev)
            yield
            for i, emit_tile in enumerate(capture_tiles):
                emit_tile()
                if i % 3 == 2:
                    yield
            ev = sV.inc(VD.tensor_reduce(mvq[:, 0:1], sgn[:, :],
                                                AX.X, OP.add))
            sV.wait(nc.gpsimd, ev)
            eg = sGP.inc(nc.gpsimd.tensor_reduce(crow2[:, 0:1], mvq[:, 0:1],
                                                 AX.C, OP.add))
            sGP.wait(nc.vector, eg)
            sV.inc(VD.tensor_scalar(
                qrow[:, 4:5], crow2[:, 0:1], float(mloc), 0.5,
                OP.add, OP.mult))
            sV.inc(VD.tensor_reduce(
                l2a[:, :], l1max[:, 0:n1c].rearrange("p (a b) -> p a b", b=l2g),
                AX.X, OP.max))
            sV.inc(VD.tensor_reduce(
                l2b[:, :], l1min[:, 0:n1c].rearrange("p (a b) -> p a b", b=l2g),
                AX.X, OP.min))
            sV.inc(VD.tensor_reduce(
                l2c[:, :], l1cnt[:, 0:n1c].rearrange("p (a b) -> p a b", b=l2g),
                AX.X, OP.add))
            sV.inc(VD.tensor_scalar(
                l2e[:, :], l2c[:, :], 2.0, 0.0, OP.subtract, OP.max))
            sV.inc(VD.tensor_scalar(
                l2e[:, :], l2e[:, :], 1.0, None, OP.add))
            sV.inc(VD.tensor_scalar(
                l2c2[:, :], l2c[:, :], 2.0, None, OP.is_ge))
            sV.inc(VD.tensor_scalar(qrow[:, 0:1], qrow[:, 12:13], 0.0, None, OP.add))
            sV.inc(VD.tensor_scalar(qrow[:, 1:2], qrow[:, 13:14], 0.0, None, OP.add))
            sV.inc(VD.memset(qrow[:, 5:6], rtop))
            yield
            for _ in range(7):
                gen_thresholds(0, 1, 16, 0, 17.0)
                row_to_dram(trow[:, 0:16], 16)
                eb = dram_bcast(tbc[:, 0:16], 16)
                sGQ.wait(nc.vector, eb)
                sV.inc(VD.tensor_scalar(
                    tbcA[:, :], tbc[:, :], qs[:, 0:1], None, OP.subtract))
                sV.inc(VD.tensor_scalar(
                    tbcB[:, :], tbc[:, :], qs[:, 1:2], None, OP.subtract))
                for j in range(16):
                    sV.inc(VD.scalar_tensor_tensor(
                        junk[:, 0:256], l2a[:, :], tbcA[:, j:j + 1], l2e[:, :],
                        OP.is_gt, OP.mult))
                    sV.inc(VD.tensor_scalar(
                        junk[:, 256:512], junk[:, 0:256], 0.0, None,
                        OP.add, OP.add, accum_out=acc[:, j:j + 1]))
                    sV.inc(VD.scalar_tensor_tensor(
                        junk[:, 0:256], l2b[:, :], tbcB[:, j:j + 1],
                        l2c2[:, :], OP.is_gt, OP.mult))
                    sV.inc(VD.tensor_scalar(
                        junk[:, 256:512], junk[:, 0:256], 0.0, None,
                        OP.add, OP.add, accum_out=acc[:, 16 + j:17 + j]))
                    if j == 7:
                        yield
                ev = sV.inc(VD.tensor_scalar(fnv[:, 0:8], fnv[:, 0:8], 0.0, None, OP.add))
                sV.wait(nc.gpsimd, ev)
                eg = sGP.inc(nc.gpsimd.tensor_reduce(
                    crow2[:, 0:32], acc[:, 0:32], AX.C, OP.add))
                sGP.wait(nc.vector, eg)
                sV.inc(VD.tensor_tensor(
                    crow2[:, 0:16], crow2[:, 0:16], crow2[:, 16:32], OP.add))
                sV.inc(VD.tensor_scalar(
                    crow2[:, 0:16], crow2[:, 0:16], qrow[:, 4:5], None,
                    OP.add))
                g = cc_start(16, OP.add, crow2[:, 0:16])
                yield
                sGQ.wait(nc.vector, g)
                update_bracket(0, 1, slice(0, 16), slice(0, 16), qrow[:, 5:6])
            sV.inc(VD.tensor_tensor(
                trow[:, 0:1], qrow[:, 0:1], qrow[:, 12:13], OP.subtract))
            sV.inc(VD.tensor_tensor(
                trow[:, 1:2], qrow[:, 0:1], qrow[:, 13:14], OP.subtract))
            row_to_dram(trow[:, 0:2], 2)
            eb = dram_bcast(qs[:, 5:7], 2)
            sGQ.wait(nc.vector, eb)
            for (arr, shcol, addcol, oc) in ((l1max, 5, 0, 0),
                                             (l1min, 6, 1, 1)):
                sV.inc(VD.tensor_scalar(
                    xscr[:, 0:n1c], arr[:, 0:n1c], qs[:, shcol:shcol + 1],
                    None, OP.is_gt))
                sV.inc(VD.tensor_scalar(
                    xscr[:, 0:n1c], xscr[:, 0:n1c], -BIG, BIG,
                    OP.mult, OP.add))
                sV.inc(VD.scalar_tensor_tensor(
                    xscr[:, 0:n1c], arr[:, 0:n1c], qs[:, addcol:addcol + 1],
                    xscr[:, 0:n1c], OP.add, OP.add))
                sV.inc(VD.tensor_reduce(
                    mvq[:, oc:oc + 1], xscr[:, 0:n1c], AX.X, OP.min))
            sV.inc(VD.tensor_tensor(
                mvq[:, 0:1], mvq[:, 0:1], mvq[:, 1:2], OP.min))
            ev = sV.inc(VD.tensor_scalar(
                mvq[:, 1:2], mvq[:, 0:1], -1.0, None, OP.mult))
            sV.wait(nc.gpsimd, ev)
            sGP.inc(nc.gpsimd.tensor_reduce(crow2[:, 0:1], mvq[:, 1:2],
                                            AX.C, OP.max))
            g = cc_start(1, OP.max, crow2[:, 0:1])
            yield
            sGQ.wait(nc.vector, g)
            sV.inc(VD.tensor_scalar(
                qrow[:, 10:11], crow[:, 0:1], -1.0, EPS, OP.mult, OP.max))
            sV.inc(VD.tensor_tensor(
                qrow[:, 10:11], qrow[:, 10:11], qrow[:, 14:15], OP.min))
            sV.inc(VD.reciprocal(qrow[:, 3:4], qrow[:, 10:11]))
            sV.inc(VD.tensor_scalar(
                trow[:, 0:1], qrow[:, 3:4], 127.0, None, OP.mult))
            sV.inc(VD.tensor_scalar(
                trow[:, 1:2], qrow[:, 10:11], 1.0 / 127.0, None, OP.mult))
            row_to_dram(trow[:, 0:2], 2)
            eb = dram_bcast(qs[:, s_cols[0]:s_cols[0] + 2], 2)
            sGQ.wait(nc.vector, eb)
            sV.inc(VD.tensor_tensor(
                qs[:, s_cols[1]:s_cols[1] + 1],
                qs[:, s_cols[1]:s_cols[1] + 1],
                gsc[:, g_col:g_col + 1], OP.mult))
            if dbg_sp is not None:
                ev = sV.inc(VD.tensor_scalar(fnv[:, 0:8], fnv[:, 0:8], 0.0, None, OP.add))
                sV.wait(nc.gpsimd, ev)
                sGQ.inc(nc.gpsimd.dma_start(out=dbg_sp[:, 0:32],
                                            in_=qrow[:, 0:32]))
            yield

        q1sc = contextlib.ExitStack()
        open_stacks.append(q1sc)
        ss1 = sb("ss1", (128, 1024), stk=q1sc)
        q1maxS = sb("q1maxS", (128, 1024), stk=q1sc)
        q1minS = sb("q1minS", (128, 1024), stk=q1sc)
        q1cnt = sb("q1cnt", (128, 1024), stk=q1sc)
        qc1h = sb("qc1h", (128, H), stk=q1sc)
        qc1a = sb("qc1a", (128, H), stk=q1sc)
        qc1m = sb("qc1m", (128, H), stk=q1sc)
        qc1u = sb("qc1u", (128, H), stk=q1sc)
        p1 = contextlib.ExitStack()
        open_stacks.append(p1)
        lnw_bc = sb("lnw_bc", (128, H), stk=p1)
        lnb_bc = sb("lnb_bc", (128, H), stk=p1)
        g_bc = sb("g_bc", (128, H), stk=p1)
        b_bc = sb("b_bc", (128, H), stk=p1)
        prow = sb("prow", (1, H), stk=p1)
        ones_col = sb("ones_col", (1, 128), stk=p1)
        xt = sb("xt", (128, 2, H), stk=p1)
        stats = sb("stats", (128, 12), stk=p1)
        mv = sb("mv", (128, 8), stk=p1)
        xnt = sb("xnt", (128, 2, H), stk=p1)
        xn2t = sb("xn2t", (128, H), stk=p1)
        hbt = sb("hbt", (128, 2, 2, H), BF16, stk=p1)
        h1t = sb("h1t", (128, 2, H), stk=p1)
        w1g = sb("w1g", (128, 2, F1), stk=p1)
        g1a = sb("g1a", (128, 8), stk=p1)
        p1ps = contextlib.ExitStack()
        open_stacks.append(p1ps)
        pbc = pst("pbc", (128, 512), p1ps)

        nc.gpsimd.memset(ones_col[:, :], 1.0)
        ego = sGP.inc(nc.gpsimd.memset(fng[:, 0:8], 0.0))
        sGP.wait(nc.tensor, ego)
        ev_evac = 0
        for k, (src, dst) in enumerate(((ln_w_d, lnw_bc), (ln_b_d, lnb_bc),
                                        (ffn_g_d, g_bc), (ffn_b_d, b_bc))):
            if k:
                nc.sync.wait_ge(sPE.h, sPE.n)
            el = sSY.inc(nc.sync.dma_start(out=prow[:, :], in_=src[:, :]))
            sSY.wait(nc.tensor, el)
            for q in range(2):
                if ev_evac:
                    sV.wait(nc.tensor, ev_evac)
                ep = sPE.inc(nc.tensor.matmul(
                    pbc[:, :], ones_col[:, :], prow[:, q * 512:(q + 1) * 512],
                    start=True, stop=True))
                sPE.wait(nc.vector, ep)
                ev_evac = sV.inc(VD.tensor_copy(
                    dst[:, q * 512:(q + 1) * 512], pbc[:, :]))

        vread_x = [0, 0]
        h1_freed = [0, 0]
        spw_hb = [0, 0]
        vread_w1g = [0, 0]
        xn_spl = [0, 0]
        for rt in range(NT):
            bi = rt % 2
            own = rt < NTO
            nc.sync.wait_ge(sV.h, vread_x[bi])
            el = sSY.inc(nc.sync.dma_start(
                out=xt[:, bi, :], in_=x_in[rt * 128:(rt + 1) * 128, :]))
            sSY.wait(nc.vector, el)
            xv = xt[:, bi, :]
            if rt % 4 == 0 and rt // 4 < 8:
                dc = rt // 4
                wb = dc % 2
                nc.scalar.wait_ge(sV.h, vread_w1g[wb])
                eq = sSQ.inc(nc.scalar.dma_start(
                    out=w1g[:, wb, :], in_=w1T_d[dc * 128:(dc + 1) * 128, :]))
                sSQ.wait(nc.vector, eq)
                vread_w1g[wb] = sV.inc(VD.tensor_reduce(
                    g1a[:, dc:dc + 1], w1g[:, wb, :], AX.X, OP.add,
                    apply_absolute_value=True))
            for pass_ in range(2):
                src = xv if pass_ == 0 else xnt[:, bi, :]
                sV.inc(VD.bn_stats(stats[:, 0:6], src[:, 0:512]))
                sV.inc(VD.bn_stats(stats[:, 6:12], src[:, 512:1024]))
                sV.inc(VD.bn_aggr(mv[:, 4:6], stats[:, 0:6]))
                sV.inc(VD.bn_aggr(mv[:, 6:8], stats[:, 6:12]))
                sV.inc(VD.tensor_tensor(
                    mv[:, 0:1], mv[:, 4:5], mv[:, 6:7], OP.add))
                sV.inc(VD.tensor_scalar(
                    mv[:, 0:1], mv[:, 0:1], 0.5, None, OP.mult))
                sV.inc(VD.tensor_tensor(
                    mv[:, 1:2], mv[:, 4:5], mv[:, 6:7], OP.subtract))
                sV.inc(VD.tensor_tensor(
                    mv[:, 1:2], mv[:, 1:2], mv[:, 1:2], OP.mult))
                sV.inc(VD.tensor_tensor(
                    mv[:, 2:3], mv[:, 5:6], mv[:, 7:8], OP.add))
                sV.inc(VD.tensor_scalar(
                    mv[:, 2:3], mv[:, 2:3], 0.5, None, OP.mult))
                sV.inc(VD.tensor_scalar(
                    mv[:, 1:2], mv[:, 1:2], 0.25, None, OP.mult))
                sV.inc(VD.tensor_tensor(
                    mv[:, 1:2], mv[:, 1:2], mv[:, 2:3], OP.add))
                ev = sV.inc(VD.tensor_scalar(
                    mv[:, 2:3], mv[:, 1:2], EPS, None, OP.add))
                sV.wait(nc.scalar, ev)
                esq = sSC.inc(SD.activation(mv[:, 2:3], mv[:, 2:3],
                                                   AF.Sqrt))
                sSC.wait(nc.vector, esq)
                sV.inc(VD.reciprocal(mv[:, 3:4], mv[:, 2:3]))
                dst = xnt[:, bi, :] if pass_ == 0 else xn2t[:, :]
                if pass_ == 0:
                    nc.vector.wait_ge(sSY.h, xn_spl[bi])
                ev = sV.inc(VD.tensor_scalar(
                    dst, src, mv[:, 0:1], mv[:, 3:4], OP.subtract, OP.mult))
                if pass_ == 0:
                    vread_x[bi] = ev
                    if own:
                        sV.wait(nc.sync, ev)
                        xn_spl[bi] = sSY.inc(nc.sync.dma_start(
                            out=xn_sp[rt * 128:(rt + 1) * 128, :], in_=dst))
            if own:
                nc.vector.wait_ge(sSY.h, h1_freed[bi])
                hv = h1t[:, bi, :]
                sV.inc(VD.tensor_tensor(hv, xn2t[:, :], g_bc[:, :],
                                               OP.mult))
                ev = sV.inc(VD.tensor_tensor(hv, hv, b_bc[:, :],
                                                    OP.add))
                sV.wait(nc.sync, ev)
                h1_freed[bi] = sSY.inc(nc.sync.dma_start(
                    out=h1_sp[rt * 128:(rt + 1) * 128, :], in_=hv))
                sV.inc(VD.scalar_tensor_tensor(
                    ss1[:, rt * 64:(rt + 1) * 64], hv[:, 0:H:16], -1.0,
                    hv[:, 0:H:16], OP.mult, OP.max))
            sV.inc(VD.tensor_tensor(
                xn2t[:, :], xn2t[:, :], lnw_bc[:, :], OP.mult))
            sV.inc(VD.tensor_tensor(
                xn2t[:, :], xn2t[:, :], lnb_bc[:, :], OP.add))
            nc.vector.wait_ge(sSY.h, spw_hb[bi])
            hh = hbt[:, bi, 0, :]
            hl = hbt[:, bi, 1, :]
            sV.inc(VD.tensor_copy(hh, xn2t[:, :]))
            ev = sV.inc(VD.tensor_tensor(hl, xn2t[:, :], hh,
                                                OP.subtract))
            sV.wait(nc.sync, ev)
            sSY.inc(nc.sync.dma_start(
                out=xn2h_sp[rt * 128:(rt + 1) * 128, :], in_=hh))
            spw_hb[bi] = sSY.inc(nc.sync.dma_start(
                out=xn2l_sp[rt * 128:(rt + 1) * 128, :], in_=hl))
        if "ssdbg0_sp" in DEBUG:
            ev = sV.inc(VD.tensor_scalar(fnv[:, 0:8], fnv[:, 0:8], 0.0, None, OP.add))
            sV.wait(nc.sync, ev)
            sSY.inc(nc.sync.dma_start(out=ssdbg0_sp[:, :], in_=ss1[:, :]))
        ev = sV.inc(VD.tensor_reduce(mvq[:, 0:1], g1a[:, 0:8],
                                            AX.X, OP.add))
        sV.wait(nc.gpsimd, ev)
        eg = sGP.inc(nc.gpsimd.tensor_reduce(crow2[:, 0:1], mvq[:, 0:1],
                                             AX.C, OP.add))
        sGP.wait(nc.vector, eg)
        sV.inc(VD.tensor_scalar(
            qrow[:, 11:12], crow2[:, 0:1], 1.0 / float(H * F1), EPS,
            OP.mult, OP.add))
        sV.inc(VD.reciprocal(trow[:, 0:1], qrow[:, 11:12]))
        sV.inc(VD.tensor_scalar(trow[:, 1:2], qrow[:, 11:12], 0.0, None, OP.add))
        row_to_dram(trow[:, 0:2], 2)
        eb = dram_bcast(gsc[:, 0:2], 2)
        sGQ.wait(nc.vector, eb)
        p1_sy_done = sSY.n
        p1_v_done = sV.n
        _close(p1ps)
        _close(p1)
        maybe_stop(2)

        cap1_state = {"vfree": [0, 0]}

        def mk_cap1(rt):
            def emit():
                bi = 0
                nc.scalar.wait_ge(sV.h, cap1_state["vfree"][bi])
                nc.scalar.wait_ge(sSY.h, p1_sy_done)
                eq = sSQ.inc(nc.scalar.dma_start(
                    out=qc1h[:, :], in_=h1_sp[rt * 128:(rt + 1) * 128, :]))
                sSQ.wait(nc.scalar, eq)
                sSC.inc(SD.activation(qc1a[:, :], qc1h[:, :],
                                             AF.Abs))
                esg = sSC.inc(SD.activation(
                    qc1m[:, :], qc1a[:, :], AF.Sign, bias=qs[:, 2:3],
                    accum_out=sgn[:, rt:rt + 1]))
                sSC.wait(nc.vector, esg)
                a = qc1a[:, :]
                m = qc1m[:, :]
                sV.inc(VD.tensor_scalar(m, a, qs[:, 0:1], None,
                                               OP.is_gt))
                sV.inc(VD.scalar_tensor_tensor(
                    m, a, qs[:, 1:2], m, OP.is_le, OP.mult))
                sV.inc(VD.scalar_tensor_tensor(
                    qc1u[:, :], a, qs[:, 0:1], m, OP.subtract, OP.mult))
                sV.inc(VD.tensor_reduce(
                    q1maxS[:, rt * 64:(rt + 1) * 64],
                    qc1u[:, :].rearrange("p (a b) -> p a b", b=16),
                    AX.X, OP.max))
                sV.inc(VD.tensor_reduce(
                    q1cnt[:, rt * 64:(rt + 1) * 64],
                    m.rearrange("p (a b) -> p a b", b=16), AX.X, OP.add))
                sV.inc(VD.scalar_tensor_tensor(
                    qc1u[:, :], a, qs[:, 1:2], m, OP.subtract, OP.mult))
                ev2 = sV.inc(VD.tensor_reduce(
                    q1minS[:, rt * 64:(rt + 1) * 64],
                    qc1u[:, :].rearrange("p (a b) -> p a b", b=16),
                    AX.X, OP.min))
                cap1_state["vfree"][bi] = ev2
            return emit

        q1gen = quantile_chain(
            ss1, 1024, SS1_RA, SS1_RB, R1, TO * H,
            q1maxS, q1minS, q1cnt, 1024, 4,
            [mk_cap1(rt) for rt in range(NTO)], (3, 4), 1, qc1u, qdbg1_sp)
        q1_live = [True]

        def q1_step():
            if q1_live[0]:
                try:
                    next(q1gen)
                except StopIteration:
                    q1_live[0] = False
            return q1_live[0]

        base_sp = scr("base_sp", (T, S))
        p2 = contextlib.ExitStack()
        open_stacks.append(p2)
        uvbv_bc = sb("uvbv_bc", (128, 2176), stk=p2)
        pro2 = sb("pro2", (1, 2176), stk=p2)
        one2 = sb("one2", (1, 128), stk=p2)
        whi = sb("whi", (128, 8 * 2176), BF16, stk=p2)
        wlo = sb("wlo", (128, 8 * 2176), BF16, stk=p2)
        wf32 = sb("wf32", (128, 2176), stk=p2)
        x2h = sb("x2h", (128, 2, H), BF16, stk=p2)
        x2l = sb("x2l", (128, 2, H), BF16, stk=p2)
        xTg = sb("xTg", (128, 2, 1024), BF16, stk=p2)
        v32 = sb("v32", (128, 2048), stk=p2)
        vsil = sb("vsil", (128, 2048), stk=p2)
        vhb = sb("vhb", (128, 2, 2048), BF16, stk=p2)
        vlb = sb("vlb", (128, 2, 2048), BF16, stk=p2)
        bsil = sb("bsil", (128, 128), stk=p2)
        bsil2 = sb("bsil2", (128, 128), stk=p2)
        g2buf = sb("g2buf", (128, 2, 1024), stk=p2)
        g2a = sb("g2a", (128, 32), stk=p2)
        p2ps = contextlib.ExitStack()
        open_stacks.append(p2ps)
        pbc2 = pst("pbc2", (128, 512), p2ps)
        psTT = [pst(f"psTT{i}", (128, 1024), p2ps, BF16) for i in range(2)]
        psV = [pst(f"psV{i}", (128, 512), p2ps) for i in range(4)]
        psBse = pst("psBse", (128, 128), p2ps)

        nc.gpsimd.wait_ge(sV.h, p1_v_done)
        nc.gpsimd.memset(one2[:, :], 1.0)
        ego = sGP.inc(nc.gpsimd.memset(fng[:, 0:8], 0.0))
        sGP.wait(nc.tensor, ego)

        def bias_bcast(col0, width, dst, nq):
            nonlocal_ev = [0]
            nc.sync.wait_ge(sPE.h, sPE.n)
            el = sSY.inc(nc.sync.dma_start(out=pro2[:, 0:width],
                                           in_=uv_b_d[:, col0:col0 + width]))
            sSY.wait(nc.tensor, el)
            eve = 0
            for q in range(nq):
                w = min(512, width - q * 512)
                if eve:
                    sV.wait(nc.tensor, eve)
                ep = sPE.inc(nc.tensor.matmul(
                    pbc2[:, 0:w], one2[:, :], pro2[:, q * 512:q * 512 + w],
                    start=True, stop=True))
                sPE.wait(nc.vector, ep)
                eve = sV.inc(VD.tensor_copy(
                    dst[:, q * 512:q * 512 + w], pbc2[:, 0:w]))
            return eve

        ev_bias_v = bias_bcast(2048, 2176, uvbv_bc, 5)

        def load_w(col0, ncols):
            evl = 0
            for dc in range(8):
                nc.sync.wait_ge(sV.h, sV.n)
                el = sSY.inc(nc.sync.dma_start(
                    out=wf32[:, 0:ncols],
                    in_=uv_wT_d[dc * 128:(dc + 1) * 128, col0:col0 + ncols]))
                sSY.wait(nc.vector, el)
                sV.inc(VD.tensor_copy(
                    whi[:, dc * ncols:(dc + 1) * ncols], wf32[:, 0:ncols]))
                evl = sV.inc(VD.tensor_tensor(
                    wlo[:, dc * ncols:(dc + 1) * ncols], wf32[:, 0:ncols],
                    whi[:, dc * ncols:(dc + 1) * ncols], OP.subtract))
            return evl

        w_ready_v = load_w(2048, 2176)

        st2 = {"x2f": [0, 0], "psT_free": [0, 0], "xT_ev": [0, 0],
               "evac_v": 0, "spl": [0, 0], "sil_free": 0, "v32_free": 0,
               "b_free": 0, "u_spill": 0, "bspill": 0}

        def p2_tile(tt, ncols, psums, out_emit, hook=None):
            bi = tt % 2
            if tt == 0:
                nc.sync.wait_ge(sSY.h, p1_sy_done)
            nc.sync.wait_ge(sPE.h, st2["x2f"][bi])
            sSY.inc(nc.sync.dma_start(
                out=x2h[:, bi, :], in_=xn2h_sp[tt * 128:(tt + 1) * 128, :]))
            elx = sSY.inc(nc.sync.dma_start(
                out=x2l[:, bi, :], in_=xn2l_sp[tt * 128:(tt + 1) * 128, :]))
            nch = (ncols + 511) // 512
            for grp in range(2):
                gb = grp % 2
                sSY.wait(nc.tensor, elx)
                sV.wait(nc.tensor, st2["psT_free"][gb])
                for dci in range(4):
                    dc = grp * 4 + dci
                    sPE.inc(nc.tensor.transpose(
                        psTT[gb][:, dci * 256:dci * 256 + 128],
                        x2h[:, bi, dc * 128:(dc + 1) * 128], identb[:, :]))
                    etp = sPE.inc(nc.tensor.transpose(
                        psTT[gb][:, dci * 256 + 128:dci * 256 + 256],
                        x2l[:, bi, dc * 128:(dc + 1) * 128], identb[:, :]))
                sPE.wait(nc.vector, etp)
                st2["xT_ev"][gb] = sV.inc(VD.tensor_copy(
                    xTg[:, gb, :], psTT[gb][:, :]))
                st2["psT_free"][gb] = st2["xT_ev"][gb]
                sV.wait(nc.tensor, st2["xT_ev"][gb])
                if grp == 0:
                    nc.tensor.wait_ge(sV.h, max(w_ready_v, st2["evac_v"],
                                                ev_bias_v))
                for dci in range(4):
                    dc = grp * 4 + dci
                    xh_t = xTg[:, gb, dci * 256:dci * 256 + 128]
                    xl_t = xTg[:, gb, dci * 256 + 128:dci * 256 + 256]
                    st = (dc == 0)
                    sp = (dc == 7)
                    for ci in range(nch):
                        w = min(512, ncols - ci * 512)
                        ws = slice(dc * ncols + ci * 512,
                                   dc * ncols + ci * 512 + w)
                        sPE.inc(nc.tensor.matmul(
                            psums[ci][:, 0:w], xh_t, whi[:, ws],
                            start=st, stop=False))
                        sPE.inc(nc.tensor.matmul(
                            psums[ci][:, 0:w], xh_t, wlo[:, ws],
                            start=False, stop=False))
                        empp = sPE.inc(nc.tensor.matmul(
                            psums[ci][:, 0:w], xl_t, whi[:, ws],
                            start=False, stop=sp))
                if hook is not None:
                    hook()
            st2["x2f"][bi] = empp
            sPE.wait(nc.vector, empp)
            out_emit(tt)

        def p2a_out(tt):
            nc.vector.wait_ge(sSC.h, st2["v32_free"])
            for ci in range(4):
                ev = sV.inc(VD.tensor_tensor(
                    v32[:, ci * 512:(ci + 1) * 512], psV[ci][:, :],
                    uvbv_bc[:, ci * 512:(ci + 1) * 512], OP.add))
            nc.scalar.wait_ge(sV.h, ev)
            nc.vector.wait_ge(sSC.h, st2["b_free"])
            evb = sV.inc(VD.tensor_tensor(
                bsil[:, :], psBse[:, :], uvbv_bc[:, 2048:2176], OP.add))
            st2["evac_v"] = evb
            nc.scalar.wait_ge(sV.h, st2["sil_free"])
            esl = sSC.inc(SD.activation(vsil[:, :], v32[:, :], AF.Silu))
            st2["v32_free"] = esl
            vb = tt % 2
            sSC.wait(nc.vector, esl)
            nc.vector.wait_ge(sSY.h, st2["spl"][vb])
            sV.inc(VD.tensor_copy(vhb[:, vb, :], vsil[:, :]))
            ev = sV.inc(VD.tensor_tensor(
                vlb[:, vb, :], vsil[:, :], vhb[:, vb, :], OP.subtract))
            st2["sil_free"] = ev
            sV.wait(nc.sync, ev)
            sSY.inc(nc.sync.dma_start(
                out=vh_sp[tt * 128:(tt + 1) * 128, :], in_=vhb[:, vb, :]))
            st2["spl"][vb] = sSY.inc(nc.sync.dma_start(
                out=vl_sp[tt * 128:(tt + 1) * 128, :], in_=vlb[:, vb, :]))
            sV.wait(nc.scalar, st2["evac_v"])
            nc.scalar.wait_ge(sSY.h, st2["bspill"])
            esb = sSC.inc(SD.activation(bsil2[:, :], bsil[:, :],
                                               AF.Silu))
            st2["b_free"] = esb
            sSC.wait(nc.sync, esb)
            st2["bspill"] = sSY.inc(nc.sync.dma_start(
                out=base_sp[tt * 128:(tt + 1) * 128, :], in_=bsil2[:, :]))

        for tt in range(NT):
            p2_tile(tt, 2176, psV + [psBse], p2a_out, hook=q1_step)

        nc.vector.wait_ge(sPE.h, sPE.n)
        ev_bias_v = bias_bcast(0, 2048, uvbv_bc, 4)
        w_ready_v = load_w(0, 2048)
        g2_reads = [0, 0]

        def g2_chunk(c):
            wb = c % 2
            nc.scalar.wait_ge(sV.h, g2_reads[wb])
            eq = sSQ.inc(nc.scalar.dma_start(
                out=g2buf[:, wb, :], in_=w2T_d[c * 128:(c + 1) * 128, :]))
            sSQ.wait(nc.vector, eq)
            g2_reads[wb] = sV.inc(VD.tensor_reduce(
                g2a[:, c:c + 1], g2buf[:, wb, :], AX.X, OP.add,
                apply_absolute_value=True))

        def p2b_out(tt):
            nc.vector.wait_ge(sSC.h, st2["v32_free"])
            for ci in range(4):
                ev = sV.inc(VD.tensor_tensor(
                    v32[:, ci * 512:(ci + 1) * 512], psV[ci][:, :],
                    uvbv_bc[:, ci * 512:(ci + 1) * 512], OP.add))
            st2["evac_v"] = ev
            nc.scalar.wait_ge(sV.h, ev)
            nc.scalar.wait_ge(sSY.h, st2["u_spill"])
            esl = sSC.inc(SD.activation(vsil[:, :], v32[:, :], AF.Silu))
            st2["v32_free"] = esl
            sSC.wait(nc.sync, esl)
            st2["u_spill"] = sSY.inc(nc.sync.dma_start(
                out=u_sp[tt * 128:(tt + 1) * 128, :], in_=vsil[:, :]))

        for tt in range(NTO):
            p2_tile(tt, 2048, psV, p2b_out, hook=q1_step)
            g2_chunk(2 * tt)
            g2_chunk(2 * tt + 1)
        ev = sV.inc(VD.tensor_reduce(mvq[:, 0:1], g2a[:, :],
                                            AX.X, OP.add))
        sV.wait(nc.gpsimd, ev)
        eg = sGP.inc(nc.gpsimd.tensor_reduce(crow2[:, 0:1], mvq[:, 0:1],
                                             AX.C, OP.add))
        sGP.wait(nc.vector, eg)
        sV.inc(VD.tensor_scalar(
            qrow[:, 11:12], crow2[:, 0:1], 1.0 / float(H * F1), EPS,
            OP.mult, OP.add))
        sV.inc(VD.reciprocal(trow[:, 0:1], qrow[:, 11:12]))
        sV.inc(VD.tensor_scalar(trow[:, 1:2], qrow[:, 11:12], 0.0, None, OP.add))
        row_to_dram(trow[:, 0:2], 2)
        eb = dram_bcast(gsc[:, 2:4], 2)
        sGQ.wait(nc.vector, eb)
        while q1_step():
            pass
        q1_v_done = sV.n
        p2_sy = sSY.n
        nc.vector.wait_ge(sPE.h, sPE.n)
        _close(p2ps)
        _close(p2)
        _close(q1sc)
        maybe_stop(3)
        q2sc = contextlib.ExitStack()
        open_stacks.append(q2sc)
        ss2 = sb("ss2", (128, 2048), stk=q2sc)
        q2maxS = sb("q2maxS", (128, 2048), stk=q2sc)
        q2minS = sb("q2minS", (128, 2048), stk=q2sc)
        q2cnt = sb("q2cnt", (128, 2048), stk=q2sc)

        att = contextlib.ExitStack()
        open_stacks.append(att)
        kTh = sb("kTh", (128, T), BF16, stk=att)
        kTl = sb("kTl", (128, T), BF16, stk=att)
        qTh = sb("qTh", (128, TO), BF16, stk=att)
        qTl = sb("qTl", (128, TO), BF16, stk=att)
        p3 = contextlib.ExitStack()
        open_stacks.append(p3)
        gb_bc = sb("gb_bc", (128, 512), stk=p3)
        pro3 = sb("pro3", (1, 128), stk=p3)
        one3 = sb("one3", (1, 128), stk=p3)
        sct = sb("sct", (128, 2, 128), stk=p3)
        bst = sb("bst", (128, 2, 128), stk=p3)
        rp1 = sb("rp1", (128, 128), stk=p3)
        rp2 = sb("rp2", (128, 128), stk=p3)
        rf = sb("rf", (128, 2, 128), stk=p3)
        rtf = sb("rtf", (128, 2, 128), stk=p3)
        p3ps = contextlib.ExitStack()
        open_stacks.append(p3ps)
        pbc3 = pst("pbc3", (128, 128), p3ps)
        psT3 = [pst(f"psT3{i}", (128, 128), p3ps) for i in range(2)]

        nc.gpsimd.wait_ge(sV.h, q1_v_done)
        nc.gpsimd.memset(one3[:, :], 1.0)
        ego = sGP.inc(nc.gpsimd.memset(fng[:, 0:8], 0.0))
        sGP.wait(nc.tensor, ego)
        nc.tensor.wait_ge(sV.h, st2["evac_v"])
        nc.sync.wait_ge(sSY.h, p2_sy)
        nc.sync.wait_ge(sV.h, q1_v_done)
        eve = 0
        for k, (src, row, col) in enumerate(((gamma_d, 0, 0), (beta_d, 0, 1),
                                             (gamma_d, 1, 2), (beta_d, 1, 3))):
            if k:
                nc.sync.wait_ge(sPE.h, sPE.n)
            el = sSY.inc(nc.sync.dma_start(out=pro3[:, :],
                                           in_=src[row:row + 1, :]))
            sSY.wait(nc.tensor, el)
            if eve:
                sV.wait(nc.tensor, eve)
            ep = sPE.inc(nc.tensor.matmul(pbc3[:, :], one3[:, :], pro3[:, :],
                                          start=True, stop=True))
            sPE.wait(nc.vector, ep)
            eve = sV.inc(VD.tensor_copy(
                gb_bc[:, col * 128:(col + 1) * 128], pbc3[:, :]))
        sct_free = [0]
        psT3_free = [0, 0]

        def rope_one(tt, gcol, bcol, dsth, dstl, pslot):
            bsl = bst[:, 0, :]
            sV.inc(VD.tensor_tensor(
                rp1[:, :], bsl, gb_bc[:, gcol * 128:(gcol + 1) * 128],
                OP.mult))
            sV.inc(VD.tensor_tensor(
                rp1[:, :], rp1[:, :], gb_bc[:, bcol * 128:(bcol + 1) * 128],
                OP.add))
            sin = sct[:, 0, 0:64]
            cos = sct[:, 1, 0:64]
            x1 = rp1[:, 0:64]
            x2 = rp1[:, 64:128]
            f = rf[:, pslot, :]
            sV.inc(VD.tensor_tensor(rp2[:, 0:64], x1, cos, OP.mult))
            sV.inc(VD.tensor_tensor(rp2[:, 64:128], x2, sin, OP.mult))
            sV.inc(VD.tensor_tensor(f[:, 0:64], rp2[:, 0:64],
                                           rp2[:, 64:128], OP.subtract))
            sV.inc(VD.tensor_tensor(rp2[:, 0:64], x2, cos, OP.mult))
            sV.inc(VD.tensor_tensor(rp2[:, 64:128], x1, sin, OP.mult))
            ev = sV.inc(VD.tensor_tensor(
                f[:, 64:128], rp2[:, 0:64], rp2[:, 64:128], OP.add))
            sV.wait(nc.tensor, ev)
            nc.tensor.wait_ge(sV.h, psT3_free[pslot])
            ep = sPE.inc(nc.tensor.transpose(psT3[pslot][:, :], f[:, :],
                                             ident[:, :]))
            sPE.wait(nc.vector, ep)
            ev = sV.inc(VD.tensor_copy(rtf[:, pslot, :],
                                              psT3[pslot][:, :]))
            psT3_free[pslot] = ev
            sV.inc(VD.tensor_copy(dsth[:, tt * 128:(tt + 1) * 128],
                                         rtf[:, pslot, :]))
            sV.inc(VD.tensor_tensor(
                dstl[:, tt * 128:(tt + 1) * 128], rtf[:, pslot, :],
                dsth[:, tt * 128:(tt + 1) * 128], OP.subtract))

        for tt in range(NT):
            nc.sync.wait_ge(sV.h, sct_free[0])
            sSY.inc(nc.sync.dma_start(
                out=sct[:, 0, 0:64], in_=sincos_d[tt * 128:(tt + 1) * 128, :]))
            sSY.inc(nc.sync.dma_start(
                out=sct[:, 1, 0:64],
                in_=sincos_d[T + tt * 128:T + (tt + 1) * 128, :]))
            elx = sSY.inc(nc.sync.dma_start(
                out=bst[:, 0, :], in_=base_sp[tt * 128:(tt + 1) * 128, :]))
            sSY.wait(nc.vector, elx)
            rope_one(tt, 2, 3, kTh, kTl, 0)
            if tt < NTO:
                rope_one(tt, 0, 1, qTh, qTl, 1)
            sct_free[0] = sV.n
        kq_ready_v = sV.n
        _close(p3ps)
        _close(p3)
        maybe_stop(4)

        sbh = sb("sbh", (128, 32, 256), BF16, stk=att)
        sbl = sb("sbl", (128, 32, 256), BF16, stk=att)
        biasb = sb("biasb", (128, 3, 256), stk=att)
        s0t = sb("s0t", (128, 3, 256), stk=att)
        srl = sb("srl", (128, 2, 256), stk=att)
        ssq = sb("ssq", (128, 2, 256), stk=att)
        vht = sb("vht", (128, 2, 2048), BF16, stk=att)
        vlt = sb("vlt", (128, 2, 2048), BF16, stk=att)
        ut = sb("ut", (128, 2048), stk=att)
        g32 = sb("g32", (128, 2048), stk=att)
        ghb = sb("ghb", (128, 2, 2048), BF16, stk=att)
        glb = sb("glb", (128, 2, 2048), BF16, stk=att)
        w1w = contextlib.ExitStack()
        open_stacks.append(w1w)
        xq1T = sb("xq1T", (128, 8 * 1024), BF16, stk=w1w)
        h1rt = sb("h1rt", (128, H), stk=w1w)
        xq1b = sb("xq1b", (128, H), BF16, stk=w1w)
        wf1 = sb("wf1", (128, 2, 1024), stk=w1w)
        tw1 = sb("tw1", (128, 2, 1024), BF16, stk=w1w)
        hq32 = sb("hq32", (128, 1024), stk=w1w)
        h2t = sb("h2t", (128, 1024), stk=w1w)

        q2_live = [False]
        q2gen = [None]

        def q2_step():
            if q2gen[0] is None or not q2_live[0]:
                return False
            try:
                next(q2gen[0])
            except StopIteration:
                q2_live[0] = False
            return q2_live[0]

        ast = {"bias_f": [0] * 3, "s0_f": [0] * 3, "psA_f": [0] * 3,
               "v_f": [0, 0], "u_ld": [0, 0], "u_cons": [0, 0],
               "spl": [0, 0], "psB_evac": 0, "sA_done": 0}

        def emit_ib(ib, interleave):
            ibps = contextlib.ExitStack()
            psA = [pst(f"psA{ib}_{i}", (128, 256), ibps) for i in range(3)]
            for jt in range(32):
                ab = jt % 3
                nc.sync.wait_ge(sV.h, ast["bias_f"][ab])
                elb = sSY.inc(nc.sync.dma_start(
                    out=biasb[:, ab, :], in_=bias_d[ib * 32 + jt, :, :]))
                nc.tensor.wait_ge(sV.h, ast["psA_f"][ab])
                if jt == 0:
                    nc.tensor.wait_ge(sV.h, max(kq_ready_v, ast["psB_evac"]))
                sPE.inc(nc.tensor.matmul(
                    psA[ab][:, :], kTh[:, jt * 128:(jt + 1) * 128],
                    qTh[:, ib * 256:(ib + 1) * 256], start=True, stop=False))
                sPE.inc(nc.tensor.matmul(
                    psA[ab][:, :], kTh[:, jt * 128:(jt + 1) * 128],
                    qTl[:, ib * 256:(ib + 1) * 256], start=False, stop=False))
                eqk = sPE.inc(nc.tensor.matmul(
                    psA[ab][:, :], kTl[:, jt * 128:(jt + 1) * 128],
                    qTh[:, ib * 256:(ib + 1) * 256], start=False, stop=True))
                sPE.wait(nc.vector, eqk)
                sSY.wait(nc.vector, elb)
                nc.vector.wait_ge(sSC.h, ast["s0_f"][ab])
                eps0 = sV.inc(VD.scalar_tensor_tensor(
                    s0t[:, ab, :], psA[ab][:, :], INV_SQRT_S, biasb[:, ab, :],
                    OP.mult, OP.add))
                ast["psA_f"][ab] = eps0
                ast["bias_f"][ab] = eps0
                sV.wait(nc.scalar, eps0)
                sq = jt % 2
                sSC.inc(SD.activation(srl[:, sq, :], s0t[:, ab, :],
                                             AF.Relu))
                esq = sSC.inc(SD.activation(ssq[:, sq, :],
                                                   srl[:, sq, :], AF.Square))
                ast["s0_f"][ab] = esq
                sSC.wait(nc.vector, esq)
                sV.inc(VD.tensor_copy(sbh[:, jt, :], ssq[:, sq, :]))
                sV.inc(VD.tensor_tensor(
                    sbl[:, jt, :], ssq[:, sq, :], sbh[:, jt, :], OP.subtract))
            ast["sA_done"] = sV.n
            ibps.close()
            ibps2 = contextlib.ExitStack()
            psB = [pst(f"psB{ib}_{i}", (128, 2048), ibps2) for i in range(2)]
            for jt in range(32):
                vb = jt % 2
                nc.scalar.wait_ge(sPE.h, ast["v_f"][vb])
                nc.sync.wait_ge(sPE.h, ast["v_f"][vb])
                evh = sSQ.inc(nc.scalar.dma_start(
                    out=vht[:, vb, :],
                    in_=vh_sp[jt * 128:(jt + 1) * 128, :]))
                evl = sSY.inc(nc.sync.dma_start(
                    out=vlt[:, vb, :],
                    in_=vl_sp[jt * 128:(jt + 1) * 128, :]))
                sSQ.wait(nc.tensor, evh)
                sSY.wait(nc.tensor, evl)
                if jt == 0:
                    nc.tensor.wait_ge(sV.h, max(ast["sA_done"],
                                                ast["psB_evac"]))
                for it01 in range(2):
                    sh_t = sbh[:, jt, it01 * 128:(it01 + 1) * 128]
                    sl_t = sbl[:, jt, it01 * 128:(it01 + 1) * 128]
                    st = (jt == 0)
                    sp = (jt == 31)
                    for eh in range(4):
                        esl_ = slice(eh * 512, (eh + 1) * 512)
                        sPE.inc(nc.tensor.matmul(
                            psB[it01][:, esl_], sh_t, vht[:, vb, esl_],
                            start=st, stop=False))
                        sPE.inc(nc.tensor.matmul(
                            psB[it01][:, esl_], sh_t, vlt[:, vb, esl_],
                            start=False, stop=False))
                        empb = sPE.inc(nc.tensor.matmul(
                            psB[it01][:, esl_], sl_t, vht[:, vb, esl_],
                            start=False, stop=sp))
                ast["v_f"][vb] = empb
                if interleave and jt % 4 == 3:
                    q2_step()
            for it01 in range(2):
                r0 = (ib * 2 + it01) * 128
                nc.sync.wait_ge(sV.h, ast["u_cons"][0])
                eul = sSY.inc(nc.sync.dma_start(
                    out=ut[:, :], in_=u_sp[r0:r0 + 128, :]))
                sPE.wait(nc.vector, empb)
                sSY.wait(nc.vector, eul)
                nc.vector.wait_ge(sSY.h, ast["spl"][it01])
                sV.inc(VD.tensor_tensor(
                    g32[:, :], psB[it01][:, :], ut[:, :], OP.mult))
                ast["u_cons"][0] = sV.n
                if it01 == 1:
                    ast["psB_evac"] = sV.n
                sV.inc(VD.tensor_copy(ghb[:, it01, :], g32[:, :]))
                evg = sV.inc(VD.tensor_tensor(
                    glb[:, it01, :], g32[:, :], ghb[:, it01, :], OP.subtract))
                sV.wait(nc.sync, evg)
                sSY.inc(nc.sync.dma_start(
                    out=gh_sp[r0:r0 + 128, :], in_=ghb[:, it01, :]))
                ast["spl"][it01] = sSY.inc(nc.sync.dma_start(
                    out=gl_sp[r0:r0 + 128, :], in_=glb[:, it01, :]))
            ibps2.close()

        nc.scalar.wait_ge(sSY.h, p2_sy)
        nc.sync.wait_ge(sSY.h, p2_sy)
        nc.scalar.wait_ge(sV.h, kq_ready_v)
        nc.sync.wait_ge(sV.h, kq_ready_v)
        emit_ib(0, False)

        while q1_step():
            pass
        w1ps = contextlib.ExitStack()
        open_stacks.append(w1ps)
        psQ = [pst(f"psQ{i}", (128, 1024), w1ps, BF16) for i in range(2)]
        psW = [pst(f"psW{i}", (128, 1024), w1ps) for i in range(2)]
        w1st = {"h1f": [0, 0], "xq1f": [0, 0], "psQf": [0, 0],
                "wf1f": [0, 0], "tw1f": [0, 0], "psWf": [0, 0],
                "hqf": [0, 0], "h2f": [0, 0], "xq1T_rd": 0, "ss2f": 0}
        h2_spills = 0
        for th in range(2):
            for rt8 in range(8):
                rt = th * 8 + rt8
                bi = rt8 % 2
                nc.scalar.wait_ge(sV.h, w1st["h1f"][0])
                eq = sSQ.inc(nc.scalar.dma_start(
                    out=h1rt[:, :],
                    in_=h1_sp[rt * 128:(rt + 1) * 128, :]))
                sSQ.wait(nc.vector, eq)
                nc.vector.wait_ge(sPE.h, w1st["xq1f"][0])
                sV.inc(VD.tensor_scalar(
                    h1rt[:, :], h1rt[:, :], qs[:, 3:4], MAGIC,
                    OP.mult, OP.add))
                sV.inc(VD.tensor_scalar(
                    h1rt[:, :], h1rt[:, :], MAGIC, 127.0,
                    OP.subtract, OP.min))
                evq = sV.inc(VD.tensor_scalar(
                    xq1b[:, :], h1rt[:, :], -128.0, None, OP.max))
                w1st["h1f"][0] = evq
                sV.wait(nc.tensor, evq)
                if th == 0 and rt8 == 0:
                    nc.tensor.wait_ge(sV.h, ast["psB_evac"])
                nc.tensor.wait_ge(sV.h, w1st["psQf"][bi])
                for dc in range(8):
                    etq = sPE.inc(nc.tensor.transpose(
                        psQ[bi][:, dc * 128:(dc + 1) * 128],
                        xq1b[:, dc * 128:(dc + 1) * 128], identb[:, :]))
                w1st["xq1f"][0] = etq
                sPE.wait(nc.vector, etq)
                nc.vector.wait_ge(sPE.h, w1st["xq1T_rd"])
                evt = sV.inc(VD.tensor_copy(
                    bass.AP(xq1T, rt8 * 128,
                            [[8 * 1024, 128], [1024, 8], [1, 128]]),
                    psQ[bi][:, :].rearrange("p (a b) -> p a b", b=128)))
                w1st["psQf"][bi] = evt
            xq1T_ready = sV.n
            for fc in range(32):
                fb = fc % 2
                nc.sync.wait_ge(sV.h, w1st["wf1f"][fb])
                el = sSY.inc(nc.sync.dma_start(
                    out=wf1[:, fb, :],
                    in_=bass.AP(w1T_d.tensor, fc * 128,
                                [[F1, 128], [128 * F1, 8], [1, 128]])))
                sSY.wait(nc.vector, el)
                nc.vector.wait_ge(sPE.h, w1st["tw1f"][fb])
                sV.inc(VD.tensor_scalar(
                    wf1[:, fb, :], wf1[:, fb, :], gsc[:, 0:1], MAGIC,
                    OP.mult, OP.add))
                sV.inc(VD.tensor_scalar(
                    wf1[:, fb, :], wf1[:, fb, :], MAGIC, 1.0,
                    OP.subtract, OP.min))
                evw = sV.inc(VD.tensor_scalar(
                    tw1[:, fb, :], wf1[:, fb, :], -1.0, None, OP.max))
                w1st["wf1f"][fb] = evw
                sV.wait(nc.tensor, evw)
                nc.tensor.wait_ge(sV.h, max(xq1T_ready, w1st["psWf"][fb]))
                for dc in range(8):
                    for qh in range(2):
                        emw = sPE.inc(nc.tensor.matmul(
                            psW[fb][:, qh * 512:(qh + 1) * 512],
                            tw1[:, fb, dc * 128:(dc + 1) * 128],
                            xq1T[:, dc * 1024 + qh * 512:
                                 dc * 1024 + (qh + 1) * 512],
                            start=(dc == 0), stop=(dc == 7)))
                w1st["tw1f"][fb] = emw
                w1st["xq1T_rd"] = emw
                sPE.wait(nc.vector, emw)
                nc.vector.wait_ge(sSC.h, w1st["hqf"][0])
                evh = sV.inc(VD.tensor_scalar(
                    hq32[:, :], psW[fb][:, :], qs[:, 4:5], None, OP.mult))
                w1st["psWf"][fb] = evh
                sV.wait(nc.scalar, evh)
                nc.scalar.wait_ge(sSY.h, w1st["h2f"][0])
                nc.scalar.wait_ge(sV.h, w1st["ss2f"])
                egl = sSC.inc(SD.activation(
                    h2t[:, :], hq32[:, :], AF.Gelu))
                w1st["hqf"][0] = egl
                sSC.wait(nc.sync, egl)
                w1st["h2f"][0] = sSY.inc(nc.sync.dma_start(
                    out=h2_sp[fc, :, th * 1024:(th + 1) * 1024],
                    in_=h2t[:, :]))
                sSC.wait(nc.vector, egl)
                w1st["ss2f"] = sV.inc(VD.scalar_tensor_tensor(
                    ss2[:, fc * 64 + th * 32:fc * 64 + th * 32 + 32],
                    h2t[:, 0:1024:32], -1.0, h2t[:, 0:1024:32],
                    OP.mult, OP.max))
        h2_sy_done = sSY.n
        w1_v_done = sV.n
        _close(w1ps)
        _close(w1w)
        maybe_stop(5)

        q2c = contextlib.ExitStack()
        open_stacks.append(q2c)
        qc2h = sb("qc2h", (128, 2048), stk=q2c)
        qc2a = sb("qc2a", (128, 2048), stk=q2c)
        qc2m = sb("qc2m", (128, 2048), stk=q2c)
        qc2u = sb("qc2u", (128, 2048), stk=q2c)
        cap2_state = {"vfree": [0, 0]}

        def mk_cap2(fc):
            def emit():
                bi = 0
                nc.sync.wait_ge(sV.h, max(cap2_state["vfree"][bi],
                                          w1_v_done))
                nc.sync.wait_ge(sSY.h, h2_sy_done)
                eq = sSY.inc(nc.sync.dma_start(
                    out=qc2h[:, :], in_=h2_sp[fc, :, :]))
                sSY.wait(nc.scalar, eq)
                sSC.inc(SD.activation(qc2a[:, :], qc2h[:, :],
                                             AF.Abs))
                esg = sSC.inc(SD.activation(
                    qc2m[:, :], qc2a[:, :], AF.Sign, bias=qs[:, 2:3],
                    accum_out=sgn[:, fc:fc + 1]))
                sSC.wait(nc.vector, esg)
                a = qc2a[:, :]
                m = qc2m[:, :]
                sV.inc(VD.tensor_scalar(m, a, qs[:, 0:1], None,
                                               OP.is_gt))
                sV.inc(VD.scalar_tensor_tensor(
                    m, a, qs[:, 1:2], m, OP.is_le, OP.mult))
                sV.inc(VD.scalar_tensor_tensor(
                    qc2u[:, :], a, qs[:, 0:1], m, OP.subtract, OP.mult))
                sV.inc(VD.tensor_reduce(
                    q2maxS[:, fc * 64:(fc + 1) * 64],
                    qc2u[:, :].rearrange("p (a b) -> p a b", b=32),
                    AX.X, OP.max))
                sV.inc(VD.tensor_reduce(
                    q2cnt[:, fc * 64:(fc + 1) * 64],
                    m.rearrange("p (a b) -> p a b", b=32), AX.X, OP.add))
                sV.inc(VD.scalar_tensor_tensor(
                    qc2u[:, :], a, qs[:, 1:2], m, OP.subtract, OP.mult))
                ev2 = sV.inc(VD.tensor_reduce(
                    q2minS[:, fc * 64:(fc + 1) * 64],
                    qc2u[:, :].rearrange("p (a b) -> p a b", b=32),
                    AX.X, OP.min))
                cap2_state["vfree"][bi] = ev2
            return emit

        q2gen[0] = quantile_chain(
            ss2, 2048, SS2_RA, SS2_RB, R2, TO * F1,
            q2maxS, q2minS, q2cnt, 2048, 8,
            [mk_cap2(fc) for fc in range(32)], (7, 8), 3, qc2u, qdbg2_sp)
        q2_live[0] = True

        for ib in range(1, 8):
            emit_ib(ib, True)
        while q2_step():
            pass
        att_v_done = sV.n
        _close(q2c)
        _close(att)
        _close(q2sc)
        maybe_stop(6)

        nc.scalar.wait_ge(sV.h, att_v_done)
        nc.sync.wait_ge(sV.h, att_v_done)
        w2w = contextlib.ExitStack()
        open_stacks.append(w2w)
        xq2h = sb("xq2h", (128, 32 * 1024), BF16, stk=w2w)
        h2q = sb("h2q", (128, 2, 1024), stk=w2w)
        wf2 = sb("wf2", (128, 2, 32 * 128), stk=w2w)
        tw2 = sb("tw2", (128, 2, 32 * 128), BF16, stk=w2w)
        fT32 = sb("fT32", (128, 2, 1024), stk=w2w)
        w2ps = contextlib.ExitStack()
        open_stacks.append(w2ps)
        psW2 = [pst(f"psW2{i}", (128, 1024), w2ps) for i in range(2)]
        w2st = {"h2qf": [0, 0], "wf2f": [0, 0], "tw2f": [0, 0],
                "psW2f": [0, 0], "fTf": [0, 0], "xq2rd": 0}
        def w2_weights(hc):
            hb = hc % 2
            nc.sync.wait_ge(sV.h, w2st["wf2f"][hb])
            el = sSY.inc(nc.sync.dma_start(
                out=wf2[:, hb, :],
                in_=bass.AP(w2T_d.tensor, hc * 128,
                            [[H, 128], [128 * H, 32], [1, 128]])))
            sSY.wait(nc.vector, el)
            nc.vector.wait_ge(sPE.h, w2st["tw2f"][hb])
            sV.inc(VD.tensor_scalar(
                wf2[:, hb, :], wf2[:, hb, :], gsc[:, 2:3], MAGIC,
                OP.mult, OP.add))
            sV.inc(VD.tensor_scalar(
                wf2[:, hb, :], wf2[:, hb, :], MAGIC, 1.0,
                OP.subtract, OP.min))
            evw = sV.inc(VD.tensor_scalar(
                tw2[:, hb, :], wf2[:, hb, :], -1.0, None, OP.max))
            w2st["wf2f"][hb] = evw
            return evw

        def w2_evac(hc, th, emw):
            hb = hc % 2
            w2st["tw2f"][hb] = emw
            w2st["xq2rd"] = emw
            sPE.wait(nc.vector, emw)
            nc.vector.wait_ge(sSY.h, w2st["fTf"][hb])
            evf = sV.inc(VD.tensor_scalar(
                fT32[:, hb, :], psW2[hb][:, :], qs[:, 8:9], None,
                OP.mult))
            w2st["psW2f"][hb] = evf
            sV.wait(nc.sync, evf)
            w2st["fTf"][hb] = sSY.inc(nc.sync.dma_start(
                out=fT_sp[hc, :, th * 1024:(th + 1) * 1024],
                in_=fT32[:, hb, :]))

        for th in range(2):
            evw0 = w2_weights(0)
            evfc = []
            for fc in range(32):
                b2 = fc % 2
                nc.scalar.wait_ge(sV.h, w2st["h2qf"][b2])
                eq = sSQ.inc(nc.scalar.dma_start(
                    out=h2q[:, b2, :],
                    in_=h2_sp[fc, :, th * 1024:(th + 1) * 1024]))
                sSQ.wait(nc.vector, eq)
                if fc == 0:
                    nc.vector.wait_ge(sPE.h, w2st["xq2rd"])
                sV.inc(VD.tensor_scalar(
                    h2q[:, b2, :], h2q[:, b2, :], qs[:, 7:8], MAGIC,
                    OP.mult, OP.add))
                sV.inc(VD.tensor_scalar(
                    h2q[:, b2, :], h2q[:, b2, :], MAGIC, 127.0,
                    OP.subtract, OP.min))
                evq = sV.inc(VD.tensor_scalar(
                    xq2h[:, fc * 1024:(fc + 1) * 1024], h2q[:, b2, :],
                    -128.0, None, OP.max))
                w2st["h2qf"][b2] = evq
                evfc.append(evq)
                if fc == 0:
                    if th == 0:
                        nc.tensor.wait_ge(sV.h, ast["psB_evac"])
                    nc.tensor.wait_ge(sV.h, max(evw0, w2st["psW2f"][0]))
                nc.tensor.wait_ge(sV.h, evfc[fc])
                for qh in range(2):
                    emw = sPE.inc(nc.tensor.matmul(
                        psW2[0][:, qh * 512:(qh + 1) * 512],
                        tw2[:, 0, fc * 128:(fc + 1) * 128],
                        xq2h[:, fc * 1024 + qh * 512:
                             fc * 1024 + (qh + 1) * 512],
                        start=(fc == 0), stop=(fc == 31)))
            xq2_ready = evfc[31]
            w2_evac(0, th, emw)
            for hc in range(1, 8):
                hb = hc % 2
                evw = w2_weights(hc)
                sV.wait(nc.tensor, evw)
                nc.tensor.wait_ge(sV.h, max(xq2_ready, w2st["psW2f"][hb]))
                for d2 in range(32):
                    for qh in range(2):
                        emw = sPE.inc(nc.tensor.matmul(
                            psW2[hb][:, qh * 512:(qh + 1) * 512],
                            tw2[:, hb, d2 * 128:(d2 + 1) * 128],
                            xq2h[:, d2 * 1024 + qh * 512:
                                 d2 * 1024 + (qh + 1) * 512],
                            start=(d2 == 0), stop=(d2 == 31)))
                w2_evac(hc, th, emw)
        fT_done = sSY.n
        w2_v_done = sV.n
        _close(w2ps)
        _close(w2w)
        maybe_stop(7)

        ow = contextlib.ExitStack()
        open_stacks.append(ow)
        owh = sb("owh", (128, 16 * 1024), BF16, stk=ow)
        owl = sb("owl", (128, 16 * 1024), BF16, stk=ow)
        owf = sb("owf", (128, 2, 1024), stk=ow)
        ob_bc = sb("ob_bc", (128, 1024), stk=ow)
        pro4 = sb("pro4", (1, 1024), stk=ow)
        one4 = sb("one4", (1, 128), stk=ow)
        ghr = sb("ghr", (128, 2, 2048), BF16, stk=ow)
        glr = sb("glr", (128, 2, 2048), BF16, stk=ow)
        gTh = sb("gTh", (128, 2, 2048), BF16, stk=ow)
        gTl = sb("gTl", (128, 2, 2048), BF16, stk=ow)
        fft = sb("fft", (128, 2, 1024), stk=ow)
        xnr = sb("xnr", (128, 2, 1024), stk=ow)
        xr = sb("xr", (128, 2, 1024), stk=ow)
        outt = sb("outt", (128, 2, 1024), stk=ow)
        owps0 = contextlib.ExitStack()
        open_stacks.append(owps0)
        pbc4 = pst("pbc4", (128, 512), owps0)

        nc.gpsimd.wait_ge(sV.h, w2_v_done)
        nc.gpsimd.memset(one4[:, :], 1.0)
        ego = sGP.inc(nc.gpsimd.memset(fng[:, 0:8], 0.0))
        sGP.wait(nc.tensor, ego)
        nc.tensor.wait_ge(sV.h, max(w2st["psW2f"]))
        gspill_done = max(ast["spl"])
        nc.scalar.wait_ge(sSY.h, gspill_done)
        nc.sync.wait_ge(sSY.h, fT_done)
        nc.scalar.wait_ge(sV.h, w2_v_done)
        nc.sync.wait_ge(sV.h, w2_v_done)
        el = sSY.inc(nc.sync.dma_start(out=pro4[:, :], in_=o_b_d[:, :]))
        sSY.wait(nc.tensor, el)
        eve = 0
        for q in range(2):
            if eve:
                sV.wait(nc.tensor, eve)
            ep = sPE.inc(nc.tensor.matmul(
                pbc4[:, :], one4[:, :], pro4[:, q * 512:(q + 1) * 512],
                start=True, stop=True))
            sPE.wait(nc.vector, ep)
            eve = sV.inc(VD.tensor_copy(
                ob_bc[:, q * 512:(q + 1) * 512], pbc4[:, :]))
        ob_ready = eve
        _close(owps0)
        owps = contextlib.ExitStack()
        open_stacks.append(owps)
        psG = [pst(f"psG{i}", (128, 2048), owps, BF16) for i in range(2)]
        psO = pst("psO", (128, 1024), owps)
        psF = pst("psF", (128, 1024), owps)
        ow_f = [0, 0]
        for ec in range(16):
            eb_ = ec % 2
            nc.sync.wait_ge(sV.h, ow_f[eb_])
            el = sSY.inc(nc.sync.dma_start(
                out=owf[:, eb_, :], in_=o_wT_d[ec * 128:(ec + 1) * 128, :]))
            sSY.wait(nc.vector, el)
            sV.inc(VD.tensor_copy(
                owh[:, ec * 1024:(ec + 1) * 1024], owf[:, eb_, :]))
            ow_f[eb_] = sV.inc(VD.tensor_tensor(
                owl[:, ec * 1024:(ec + 1) * 1024], owf[:, eb_, :],
                owh[:, ec * 1024:(ec + 1) * 1024], OP.subtract))
        ow_ready = sV.n
        ost = {"g_f": [0, 0], "gT_f": [0, 0], "psG_f": [0, 0],
               "fft_f": [0, 0], "xn_f": [0, 0], "x_f": [0, 0],
               "out_f": [0, 0], "psO_ev": 0, "psF_ev": 0}
        for it in range(NTO):
            tb = it % 2
            nc.scalar.wait_ge(sPE.h, ost["g_f"][tb])
            sSQ.inc(nc.scalar.dma_start(
                out=ghr[:, tb, :], in_=gh_sp[it * 128:(it + 1) * 128, :]))
            eg_ = sSQ.inc(nc.scalar.dma_start(
                out=glr[:, tb, :], in_=gl_sp[it * 128:(it + 1) * 128, :]))
            sSQ.wait(nc.tensor, eg_)
            if it == 0:
                nc.tensor.wait_ge(sV.h, ob_ready)
            nc.tensor.wait_ge(sV.h, ost["psG_f"][tb])
            for ec in range(16):
                sPE.inc(nc.tensor.transpose(
                    psG[tb][:, ec * 128:(ec + 1) * 128],
                    ghr[:, tb, ec * 128:(ec + 1) * 128], identb[:, :]))
            etg = sPE.n
            ost["g_f"][tb] = etg
            sPE.wait(nc.vector, etg)
            nc.vector.wait_ge(sPE.h, ost["gT_f"][tb])
            ev1 = sV.inc(VD.tensor_copy(gTh[:, tb, :], psG[tb][:, :]))
            sV.wait(nc.tensor, ev1)
            nc.tensor.wait_ge(sV.h, ev1)
            for ec in range(16):
                sPE.inc(nc.tensor.transpose(
                    psG[tb][:, ec * 128:(ec + 1) * 128],
                    glr[:, tb, ec * 128:(ec + 1) * 128], identb[:, :]))
            etg2 = sPE.n
            sPE.wait(nc.vector, etg2)
            ev2 = sV.inc(VD.tensor_copy(gTl[:, tb, :], psG[tb][:, :]))
            ost["psG_f"][tb] = ev2
            sV.wait(nc.tensor, ev2)
            if it == 0:
                nc.tensor.wait_ge(sV.h, ow_ready)
            nc.tensor.wait_ge(sV.h, ost["psO_ev"])
            for ec in range(16):
                ecs = slice(ec * 128, (ec + 1) * 128)
                for qh in range(2):
                    psl = slice(qh * 512, (qh + 1) * 512)
                    ews = slice(ec * 1024 + qh * 512,
                                ec * 1024 + (qh + 1) * 512)
                    sPE.inc(nc.tensor.matmul(
                        psO[:, psl], gTh[:, tb, ecs], owh[:, ews],
                        start=(ec == 0), stop=False))
                    sPE.inc(nc.tensor.matmul(
                        psO[:, psl], gTh[:, tb, ecs], owl[:, ews],
                        start=False, stop=False))
                    emo = sPE.inc(nc.tensor.matmul(
                        psO[:, psl], gTl[:, tb, ecs], owh[:, ews],
                        start=False, stop=(ec == 15)))
            ost["gT_f"][tb] = emo
            nc.sync.wait_ge(sPE.h, ost["fft_f"][tb])
            for hc in range(8):
                effl = sSY.inc(nc.sync.dma_start(
                    out=fft[:, tb, hc * 128:(hc + 1) * 128],
                    in_=fT_sp[hc, :, it * 128:(it + 1) * 128]))
            sSY.wait(nc.tensor, effl)
            nc.tensor.wait_ge(sV.h, ost["psF_ev"])
            for hc in range(8):
                etf = sPE.inc(nc.tensor.transpose(
                    psF[:, hc * 128:(hc + 1) * 128],
                    fft[:, tb, hc * 128:(hc + 1) * 128], ident[:, :]))
            ost["fft_f"][tb] = etf
            nc.sync.wait_ge(sV.h, ost["xn_f"][tb])
            sSY.inc(nc.sync.dma_start(
                out=xnr[:, tb, :], in_=xn_sp[it * 128:(it + 1) * 128, :]))
            exl = sSY.inc(nc.sync.dma_start(
                out=xr[:, tb, :], in_=x_in[it * 128:(it + 1) * 128, :]))
            sPE.wait(nc.vector, etf)
            sSY.wait(nc.vector, exl)
            nc.vector.wait_ge(sSY.h, ost["out_f"][tb])
            sV.inc(VD.tensor_tensor(outt[:, tb, :], psO[:, :],
                                           ob_bc[:, :], OP.add))
            ost["psO_ev"] = sV.n
            sV.inc(VD.tensor_tensor(outt[:, tb, :], outt[:, tb, :],
                                           psF[:, :], OP.add))
            ost["psF_ev"] = sV.n
            sV.inc(VD.tensor_tensor(outt[:, tb, :], outt[:, tb, :],
                                           xnr[:, tb, :], OP.add))
            evo = sV.inc(VD.tensor_tensor(
                outt[:, tb, :], outt[:, tb, :], xr[:, tb, :], OP.add))
            ost["xn_f"][tb] = evo
            sV.wait(nc.sync, evo)
            ost["out_f"][tb] = sSY.inc(nc.sync.dma_start(
                out=out_d[it * 128:(it + 1) * 128, :], in_=outt[:, tb, :]))
        nc.sync.wait_ge(sSY.h, sSY.n)
        _close(owps)
        _close(ow)
      except _Trunc:
        for stk in reversed(open_stacks):
            stk.close()
        nc.sync.wait_ge(sSY.h, sSY.n)
        nc.scalar.wait_ge(sSQ.h, sSQ.n)
        nc.gpsimd.wait_ge(sGQ.h, sGQ.n)
        nc.gpsimd.wait_ge(sCC.h, sCC.n)
    es.close()
    return nc


_CACHE = {}


def kernel(L0_attn_x, L0_attn_ln_w, L0_attn_ln_b, L0_attn_uv_w, L0_attn_uv_b,
           L0_attn_gamma, L0_attn_beta, L0_attn_w, L0_attn_o_w, L0_attn_o_b,
           L0_ffn_ln_g, L0_ffn_ln_b, L0_ffn_w1, L0_ffn_w2):
    f32 = np.float32
    x = np.ascontiguousarray(np.asarray(L0_attn_x, f32))
    uv_wT = np.ascontiguousarray(np.asarray(L0_attn_uv_w, f32).T)
    o_wT = np.ascontiguousarray(np.asarray(L0_attn_o_w, f32).T)
    w1T = np.ascontiguousarray(np.asarray(L0_ffn_w1, f32).T)
    w2T = np.ascontiguousarray(np.asarray(L0_ffn_w2, f32).T)
    uv_b = np.asarray(L0_attn_uv_b, f32).reshape(1, NUV)
    ln_w = np.asarray(L0_attn_ln_w, f32).reshape(1, H)
    ln_b = np.asarray(L0_attn_ln_b, f32).reshape(1, H)
    ffn_g = np.asarray(L0_ffn_ln_g, f32).reshape(1, H)
    ffn_b = np.asarray(L0_ffn_ln_b, f32).reshape(1, H)
    gamma = np.ascontiguousarray(np.asarray(L0_attn_gamma, f32))
    beta = np.ascontiguousarray(np.asarray(L0_attn_beta, f32))
    o_b = np.asarray(L0_attn_o_b, f32).reshape(1, H)
    wrel = np.asarray(L0_attn_w, f32)

    import jax
    import jax.numpy as jnp
    cpu = jax.devices("cpu")[0]
    with jax.default_device(cpu):
        half = S // 2
        inv_freq = 10000.0 ** (jnp.arange(half, dtype=jnp.float32) / half)
        pos = jnp.arange(T, dtype=jnp.float32)
        sinu = pos[:, None] * inv_freq[None, :]
        sin_t = np.asarray(jnp.sin(sinu), f32)
        cos_t = np.asarray(jnp.cos(sinu), f32)

    identf = np.eye(128, dtype=f32)
    iota16 = np.arange(1, 17, dtype=f32).reshape(1, 16)

    in_maps = []
    for c in range(8):
        b = c // 2
        halfc = c % 2
        r0 = halfc * TO
        roll = np.concatenate([np.arange(r0, T), np.arange(0, r0)])
        xc = np.ascontiguousarray(x[b][roll])
        sincos = np.ascontiguousarray(
            np.concatenate([sin_t[roll], cos_t[roll]], 0))
        jg = roll.astype(np.int64)
        ig = (r0 + np.arange(TO)).astype(np.int64)
        rel = jg[:, None] - ig[None, :] + 511
        eff = np.where(rel < 0, rel + 1023, rel)
        bias_full = np.take(wrel, np.clip(eff, 0, 1022))
        bias_t = np.ascontiguousarray(
            bias_full.reshape(32, 128, 8, 256).transpose(2, 0, 1, 3)
            .reshape(8 * 32, 128, 256))
        in_maps.append({
            "x": xc, "uv_wT": uv_wT, "uv_b": uv_b, "ln_w": ln_w,
            "ln_b": ln_b, "ffn_g": ffn_g, "ffn_b": ffn_b, "gamma": gamma,
            "beta": beta, "o_wT": o_wT, "o_b": o_b, "w1T": w1T, "w2T": w2T,
            "sincos": sincos, "bias_t": bias_t, "identf": identf,
            "iota16": iota16,
        })

    if "nc" not in _CACHE:
        _CACHE["nc"] = build_nc()
    res = run_bass_kernel_spmd(_CACHE["nc"], in_maps, core_ids=list(range(8)))
    out = np.empty((B, T, H), f32)
    for c in range(8):
        out[c // 2, (c % 2) * TO:(c % 2) * TO + TO] = res.results[c]["out"]
    return out
